# version 07 (final)
# speedup vs baseline: 1.2658x; 1.0209x over previous
import contextlib
import os

import numpy as np
import ml_dtypes

import concourse.bass as bass
import concourse.mybir as mybir
from concourse.bass_utils import run_bass_kernel_spmd

F32 = mybir.dt.float32
BF16 = mybir.dt.bfloat16
I8 = mybir.dt.int8
AF = mybir.ActivationFunctionType
ALU = mybir.AluOpType
AXX = mybir.AxisListType.X
BF = ml_dtypes.bfloat16

NCORES = 8
D = 2048
NCH = 128
CPC = 16
TPC = CPC * 128
B = 2
SEQ = 8192
EPS = 1e-5


class Script:
    def __init__(self):
        self.steps = []
        self.counts = {}

    def step(self, eng, fn, waits=(), incs=()):
        rw = []
        for w in waits:
            s, v = (w, None) if isinstance(w, str) else w
            rw.append((s, self.counts.get(s, 0) if v is None else v))
        ri = []
        for s, n in incs:
            self.counts[s] = self.counts.get(s, 0) + n
            ri.append((s, n, self.counts[s]))
        assert len(ri) <= 1, "at most one inc per step"
        self.steps.append((eng, fn, rw, ri))

    def count(self, s):
        return self.counts.get(s, 0)


_AUTO_SERIAL = ("vector", "scalar")


def emit_script(script, block, sems):
    ser_name = {"vector": "ser_v", "scalar": "ser_s"}

    def mk(name):
        def body(eng):
            waited = {}
            marker = None
            ser = 0
            for e, fn, waits, incs in script.steps:
                if e != name:
                    continue
                allw = list(waits)
                if name in _AUTO_SERIAL and marker is not None:
                    allw.append(marker)
                for s, v in allw:
                    if v > 0 and waited.get(s, 0) < v:
                        eng.wait_ge(sems[s], v)
                        waited[s] = v
                inst = fn(eng)
                if inst is None:
                    assert not incs, f"step on {name} with incs returned None"
                    continue
                if incs:
                    s, n, after = incs[0]
                    inst.then_inc(sems[s], n)
                    if name in _AUTO_SERIAL:
                        marker = (s, after)
                elif name in _AUTO_SERIAL:
                    ser += 1
                    inst.then_inc(sems[ser_name[name]], 1)
                    marker = (ser_name[name], ser)
        return body

    block.gpsimd(mk("gpsimd"))
    block.sync(mk("sync"))
    block.tensor(mk("tensor"))
    block.vector(mk("vector"))
    block.scalar(mk("scalar"))


_FUSED_EXT_IN = {
    "x0", "wbr", "wbs", "wcr", "wcm", "dvec", "afac", "identD", "onesr",
    "wg", "wu", "wd", "sg", "su", "sd", "identB", "identF",
    "wqkv", "lnsc", "lnbi", "cosl", "sinl",
    "masks", "onesc",
    "wout",
    "gw", "wg_e", "wu_e", "wd_e", "swg", "swu", "swd",
}


class Ctx:
    def __init__(self, nc, sc, sems, ar, psA, psB, mode):
        self.nc = nc
        self.sc = sc
        self.sems = sems
        self.ar = ar
        self.psA = psA
        self.psB = psB
        self.mode = mode
        self.off = 0
        self.dram = {}

    def reset_arena(self):
        self.off = 0

    def sb(self, name, shape, dtype):
        esz = mybir.dt.size(dtype)
        off = (self.off + 31) // 32 * 32
        nb = shape[1] * esz
        self.off = off + nb
        assert self.off <= self.ar.shape[1], f"arena overflow at {name}"
        view = self.ar.bitcast(dtype)
        return view[0:shape[0], off // esz: off // esz + shape[1]]

    def _dram(self, name, shape, dtype, kind):
        if name in self.dram:
            return self.dram[name]
        ap = self.nc.dram_tensor(name, shape, dtype, kind=kind).ap()
        self.dram[name] = ap
        return ap

    def di(self, name, shape, dtype):
        if self.mode == "solo" or name in _FUSED_EXT_IN:
            return self._dram(name, shape, dtype, "ExternalInput")
        return self._dram(name, shape, dtype, "Internal")

    force_ext = frozenset(
        os.environ.get("FEXT", "").split(",")) if os.environ.get("FEXT") else frozenset()

    def do(self, name, shape, dtype):
        if self.mode == "solo" or name == "x4o" or name in self.force_ext:
            return self._dram(name, shape, dtype, "ExternalOutput")
        return self._dram(name, shape, dtype, "Internal")

    def ds(self, name, shape, dtype):
        return self._dram(name, shape, dtype, "Internal")

    def p512(self, i):
        base = self.psA if i < 4 else self.psB
        return base[:, (i % 4) * 512:(i % 4) * 512 + 512]

    def p1024(self, i):
        base = [self.psA, self.psA, self.psB][i]
        off = [0, 1024, 0][i]
        return base[:, off:off + 1024]

    def pd512(self, i):
        return self.psB[0:1, 1024 + i * 512:1536 + i * 512]

    def barrier(self):
        sc = self.sc
        names = ["s_t", "s_v", "s_s", "s_o", "s_u", "s_h", "s_wd", "s_acc",
                 "s_kv", "s_r", "s_mb", "s_w", "s_cc"]
        waits = [(n, sc.count(n)) for n in names if sc.count(n) > 0]
        for eng in ["tensor", "vector", "scalar", "sync", "gpsimd"]:
            sc.step(eng, lambda e: None, waits=waits)


_ALL_SEMS = ["s_w", "s_mb", "s_u", "s_t", "s_v", "s_s", "s_o", "s_h",
             "s_wd", "s_acc", "s_kv", "s_r", "s_cc", "ser_v", "ser_s"]
_ARENA_BYTES = 212832


def _solo_build(emit_fn, *args):
    nc = bass.Bass("TRN2", target_bir_lowering=False)
    sc = Script()
    with contextlib.ExitStack() as st_:
        E = st_.enter_context
        arena = E(nc.sbuf_tensor("arena", [128, _ARENA_BYTES], I8))
        psA = E(nc.psum_tensor("psA", [128, 2048], F32))
        psB = E(nc.psum_tensor("psB", [128, 2048], F32))
        sems = {n: E(nc.semaphore(n)) for n in _ALL_SEMS}
        block = E(nc.Block())
        X = Ctx(nc, sc, sems, arena[:], psA[:], psB[:], "solo")
        emit_fn(X, *args)
        sc.step("sync", lambda eng: None, waits=[("s_o", sc.count("s_o"))])
        emit_script(sc, block, sems)
    return nc


def _shard_tokens(x):
    xc = np.ascontiguousarray(x).reshape(NCH, 128, x.shape[-1])
    return [np.ascontiguousarray(xc[i::NCORES]) for i in range(NCORES)]


def _unshard_tokens(parts):
    out = np.empty((NCH, 128, parts[0].shape[-1]), parts[0].dtype)
    for i in range(NCORES):
        out[i::NCORES] = parts[i]
    return out.reshape(B, SEQ, parts[0].shape[-1])


def _consts():
    ident = np.eye(128, dtype=BF)
    ones_row_f32 = np.ones((1, 128), np.float32)
    ones_col_bf = np.ones((128, 1), BF)
    inv = 1.0 / (10000.0 ** (np.arange(0, 128, 2, dtype=np.float32) / 128.0))
    t = np.arange(SEQ, dtype=np.float32)[:, None] * inv[None, :]
    cos, sin = np.cos(t).astype(np.float32), np.sin(t).astype(np.float32)
    trilT = np.triu(np.ones((128, 128), np.float32))
    return dict(ident=ident, ones_row_f32=ones_row_f32, ones_col_bf=ones_col_bf,
                cos=cos, sin=sin, trilT=trilT)


def emit_M_build(sc, afac_dram, ones_row_f32, arow, mwork, mscr, macol,
                 M0b, M1b, pbank):
    def cp_a7(eng):
        src = afac_dram.rearrange("o (l a b) -> (o l) a b", a=2, b=2)[7]
        return eng.dma_start(out=mwork[0][0:2, 0:2], in_=src)
    sc.step("gpsimd", cp_a7, waits=[("s_w", None)], incs=[("s_mb", 16)])

    src = 0
    for j in range(1, 8):
        m = 1 << j
        lvl = 7 - j
        def bcast(eng, lvl=lvl, m=m):
            return eng.matmul(pbank[0:m, 0:4], ones_row_f32[0:1, 0:m],
                              arow[0:1, 4 * lvl:4 * lvl + 4], start=True, stop=True)
        sc.step("tensor", bcast, waits=[("s_mb", None), ("s_w", None)],
                incs=[("s_mb", 1)])
        def evac(eng, m=m):
            return eng.tensor_copy(macol[0:m, 0:4], pbank[0:m, 0:4])
        sc.step("vector", evac, waits=[("s_mb", None)], incs=[("s_mb", 1)])
        for ab in range(4):
            def scale(eng, ab=ab, m=m, src=src):
                return eng.tensor_scalar_mul(mscr[0:m, 128 * ab:128 * ab + m],
                                             mwork[src][0:m, 0:m],
                                             macol[0:m, ab:ab + 1])
            sc.step("vector", scale, waits=[], incs=[("s_mb", 1)] if ab == 3 else [])
        if j < 7:
            for ab in range(4):
                a, bq = ab >> 1, ab & 1
                def asm(eng, ab=ab, a=a, bq=bq, m=m, dst=1 - src):
                    return eng.dma_start(
                        out=mwork[dst][a * m:(a + 1) * m, bq * m:(bq + 1) * m],
                        in_=mscr[0:m, 128 * ab:128 * ab + m])
                sc.step("gpsimd", asm, waits=[("s_mb", None)], incs=[("s_mb", 16)])
            src = 1 - src
        else:
            for ab in range(4):
                a, bq = ab >> 1, ab & 1
                def fin(eng, ab=ab, a=a, bq=bq):
                    dst = (M0b if a == 0 else M1b)
                    return eng.tensor_copy(dst[:, 128 * bq:128 * bq + 128],
                                           mscr[:, 128 * ab:128 * ab + 128])
                sc.step("vector", fin, waits=[("s_mb", None)],
                        incs=[("s_mb", 1)] if ab == 3 else [])


def emit_ln(sc, xpre, rs, nblk, junk, sumsq, stat, outs, out_waits=(), dmodel=D):
    def sq(eng):
        return eng.activation(junk[:], xpre[:], AF.Square, accum_out=sumsq[:, 0:1])
    sc.step("scalar", sq, waits=[("s_v", None)], incs=[("s_s", 1)])

    def red(eng):
        return eng.tensor_reduce(stat[:, 0:1], rs[:, 0:nblk], AXX, ALU.add)
    sc.step("vector", red, waits=[("s_s", None)])
    def mean(eng):
        return eng.tensor_scalar_mul(stat[:, 0:1], stat[:, 0:1], 1.0 / dmodel)
    sc.step("vector", mean, waits=[])
    def e2(eng):
        return eng.tensor_scalar_mul(stat[:, 1:2], sumsq[:, 0:1], 1.0 / dmodel)
    sc.step("vector", e2, waits=[("s_s", None)])
    def msq(eng):
        return eng.tensor_mul(stat[:, 2:3], stat[:, 0:1], stat[:, 0:1])
    sc.step("vector", msq, waits=[])
    def var_(eng):
        return eng.tensor_sub(stat[:, 1:2], stat[:, 1:2], stat[:, 2:3])
    sc.step("vector", var_, waits=[])
    def veps(eng):
        return eng.tensor_scalar_add(stat[:, 1:2], stat[:, 1:2], float(EPS))
    sc.step("vector", veps, waits=[], incs=[("s_v", 1)])

    def sqr(eng):
        return eng.activation(stat[:, 2:3], stat[:, 1:2], AF.Sqrt)
    sc.step("scalar", sqr, waits=[("s_v", None)], incs=[("s_s", 1)])

    def recip(eng):
        return eng.reciprocal(stat[:, 2:3], stat[:, 2:3])
    sc.step("vector", recip, waits=[("s_s", None)])
    def negmb(eng):
        return eng.tensor_scalar(stat[:, 3:4], stat[:, 0:1], stat[:, 2:3],
                                 -1.0, ALU.mult, ALU.mult)
    sc.step("vector", negmb, waits=[], incs=[("s_v", 1)])

    for oi, out_ap in enumerate(outs):
        def norm(eng, out_ap=out_ap):
            return eng.activation(out_ap, xpre[:], AF.Identity,
                                  bias=stat[:, 3:4], scale=stat[:, 2:3])
        w = [("s_v", None)] + (list(out_waits) if oi == 0 else [])
        sc.step("scalar", norm, waits=w, incs=[("s_s", 1)])


def emit_stage1(X):
    sc, nc = X.sc, X.nc
    x0 = X.di("x0", [CPC, 128, D], F32)
    wbr = X.di("wbr", [D, 64], F32)
    wbs = X.di("wbs", [64, 256], F32)
    wcr = X.di("wcr", [256, 64], F32)
    wcm = X.di("wcm", [64, D], F32)
    dvec = X.di("dvec", [1, D], F32)
    afac = X.di("afac", [1, 32], F32)
    identD = X.di("identD", [128, 128], BF16)
    onesr = X.di("onesr", [1, 128], F32)
    x1o = X.do("x1", [CPC, 128, D], F32)
    if True:
        sb = X.sb
        s_ident = sb("s_ident", [128, 128], BF16)
        s_onesr = sb("s_onesr", [1, 128], F32)
        s_arow = sb("s_arow", [1, 32], F32)
        s_wbr = sb("s_wbr", [128, 16 * 64], BF16)
        s_wbs = sb("s_wbs", [64, 256], BF16)
        s_wcr = sb("s_wcr", [128, 128], BF16)
        s_wcm = sb("s_wcm", [64, D], BF16)
        s_dp1 = sb("s_dp1", [128, D], F32)
        s_drow = sb("s_drow", [1, D], F32)
        m_w0 = sb("m_w0", [128, 256], F32)
        m_w1 = sb("m_w1", [128, 256], F32)
        m_scr = sb("m_scr", [128, 512], F32)
        m_acol = sb("m_acol", [128, 4], F32)
        s_M0 = sb("s_M0", [128, 256], BF16)
        s_M1 = sb("s_M1", [128, 256], BF16)
        u_bf = sb("u_bf", [128, 2 * 4 * D], BF16)
        uT = sb("uT", [128, 2 * 16 * 512], BF16)
        rT_s = sb("rT_s", [64, 512], BF16)
        stT_s = sb("stT_s", [128, 2 * 512], BF16)
        st2_s = sb("st2_s", [128, 2 * 512], BF16)
        cT_s = sb("cT_s", [64, 512], BF16)
        xpre = sb("xpre", [128, D], F32)
        rs = sb("rs", [128, 4], F32)
        junk = sb("junk", [128, D], F32)
        sumsq = sb("sumsq", [128, 1], F32)
        stat = sb("stat", [128, 4], F32)
        x1t = sb("x1t", [128, D], F32)
        du = sb("du", [128, D], F32)
        p0 = X.p512(0)
        p1 = X.p512(1)
        p2 = X.p512(2)
        p3 = X.p512(3)
        p4 = X.p512(4)
        p5 = X.p512(5)
        def ld(dst, src):
            def f(eng, dst=dst, src=src):
                return eng.dma_start(out=dst, in_=src)
            return f

        for dst, src in [
            (s_wbr[:].rearrange("p (c j) -> p c j", j=64),
             wbr.rearrange("(c p) j -> p c j", p=128)),
            (s_wbs[:], wbs),
            (s_wcr[:].rearrange("p (c j) -> p c j", j=64),
             wcr.rearrange("(c p) j -> p c j", p=128)),
            (s_wcm[:], wcm),
        ]:
            sc.step("gpsimd", ld(dst, src), incs=[("s_w", 16)])
        for dst, src in [(s_ident[:], identD), (s_onesr[:], onesr),
                         (s_arow[:], afac), (s_drow[:], dvec)]:
            sc.step("sync", ld(dst, src), incs=[("s_w", 16)])

        for blk in range(4):
            def bc(eng, blk=blk):
                return eng.matmul(p4[:, 0:512], s_onesr[0:1, :],
                                  s_drow[0:1, 512 * blk:512 * blk + 512],
                                  start=True, stop=True)
            sc.step("tensor", bc, waits=[("s_w", None), ("s_v", None)],
                    incs=[("s_t", 1)])
            def ev(eng, blk=blk):
                return eng.tensor_scalar_add(s_dp1[:, 512 * blk:512 * blk + 512],
                                             p4[:, 0:512], 1.0)
            sc.step("vector", ev, waits=[("s_t", None)], incs=[("s_v", 1)])

        emit_M_build(sc, afac, s_onesr, s_arow, [m_w0, m_w1], m_scr, m_acol,
                     s_M0, s_M1, p5)

        for g in range(4):
            gb = g % 2
            def ldu(eng, g=g, gb=gb):
                return eng.dma_start(
                    out=u_bf[:].rearrange("p (b c j) -> p b c j", b=2, j=D)[:, gb],
                    in_=x0.rearrange("c p j -> p c j")[:, 4 * g:4 * g + 4, :])
            sc.step("gpsimd", ldu, waits=[("s_s", None)] if g >= 2 else [],
                    incs=[("s_u", 16)])

            uTg = uT[:, gb * 8192:(gb + 1) * 8192]
            for c in range(4):
                for dg in range(4):
                    pb = [p0, p1][dg % 2]
                    for s4 in range(4):
                        d_ = dg * 4 + s4
                        def tr(eng, c=c, d_=d_, s4=s4, pb=pb, gb=gb):
                            return eng.transpose(
                                pb[:].bitcast(BF16)[:, 128 * s4:128 * s4 + 128],
                                u_bf[:, gb * 8192 + c * D + d_ * 128:
                                     gb * 8192 + c * D + d_ * 128 + 128],
                                s_ident[:])
                        w = [("s_u", None), ("s_w", None), ("s_v", None)]
                        sc.step("tensor", tr, waits=w if s4 == 0 else [],
                                incs=[("s_t", 1)] if s4 == 3 else [])
                    def tev(eng, c=c, dg=dg, pb=pb, uTg=uTg):
                        src = pb[:].bitcast(BF16)[:, 0:512].rearrange(
                            "p (s j) -> p s j", j=128)
                        dst = uTg.rearrange("p (d c j) -> p d c j", c=4, j=128)[
                            :, 4 * dg:4 * dg + 4, c, :]
                        return eng.tensor_copy(dst, src)
                    sc.step("vector", tev, waits=[("s_t", None)], incs=[("s_v", 1)])

            for d_ in range(16):
                def mm(eng, d_=d_, uTg=uTg):
                    return eng.matmul(p2[0:64, :], s_wbr[:, 64 * d_:64 * d_ + 64],
                                      uTg[:, d_ * 512:(d_ + 1) * 512],
                                      start=(d_ == 0), stop=(d_ == 15))
                sc.step("tensor", mm, waits=[("s_v", None)] if d_ == 0 else [],
                        incs=[("s_t", 1)] if d_ == 15 else [])
            def ev_r(eng):
                return eng.tensor_copy(rT_s[:], p2[0:64, :])
            sc.step("vector", ev_r, waits=[("s_t", None)], incs=[("s_v", 1)])

            for ic in range(2):
                def mm(eng, ic=ic):
                    return eng.matmul(p3[:, :], s_wbs[:, 128 * ic:128 * ic + 128],
                                      rT_s[:], start=True, stop=True)
                sc.step("tensor", mm, waits=[("s_v", None)], incs=[("s_t", 1)])
                def ev(eng, ic=ic):
                    return eng.tensor_copy(stT_s[:, 512 * ic:512 * ic + 512], p3[:, :])
                sc.step("vector", ev, waits=[("s_t", None)], incs=[("s_v", 1)])

            for jc in range(2):
                for ic in range(2):
                    def mm(eng, jc=jc, ic=ic):
                        Msrc = [s_M0, s_M1][ic]
                        return eng.matmul(p2[:, :], Msrc[:, 128 * jc:128 * jc + 128],
                                          stT_s[:, 512 * ic:512 * ic + 512],
                                          start=(ic == 0), stop=(ic == 1))
                    sc.step("tensor", mm,
                            waits=[("s_v", None), ("s_mb", None)] if ic == 0 else [],
                            incs=[("s_t", 1)] if ic == 1 else [])
                def ev(eng, jc=jc):
                    return eng.tensor_copy(st2_s[:, 512 * jc:512 * jc + 512], p2[:, :])
                sc.step("vector", ev, waits=[("s_t", None)], incs=[("s_v", 1)])

            for ic in range(2):
                def mm(eng, ic=ic):
                    return eng.matmul(p3[0:64, :], s_wcr[:, 64 * ic:64 * ic + 64],
                                      st2_s[:, 512 * ic:512 * ic + 512],
                                      start=(ic == 0), stop=(ic == 1))
                sc.step("tensor", mm, waits=[("s_v", None)] if ic == 0 else [],
                        incs=[("s_t", 1)] if ic == 1 else [])
            def ev_c(eng):
                return eng.tensor_copy(cT_s[:], p3[0:64, :])
            sc.step("vector", ev_c, waits=[("s_t", None)], incs=[("s_v", 1)])

            for ts in range(4):
                ch = 4 * g + ts
                for db in range(4):
                    pbk = [p0, p1, p2, p3][db]
                    def mm(eng, ts=ts, db=db, pbk=pbk):
                        return eng.matmul(pbk[:, :], cT_s[:, 128 * ts:128 * ts + 128],
                                          s_wcm[:, 512 * db:512 * db + 512],
                                          start=True, stop=True)
                    sc.step("tensor", mm, waits=[("s_v", None)] if db == 0 else [],
                            incs=[("s_t", 1)] if db == 3 else [])
                def dmul(eng, ts=ts, gb=gb):
                    return eng.tensor_mul(
                        du[:], u_bf[:, gb * 8192 + ts * D:gb * 8192 + ts * D + D],
                        s_dp1[:])
                sc.step("vector", dmul, waits=[("s_u", None)], incs=[])
                for db in range(4):
                    pbk = [p0, p1, p2, p3][db]
                    def ev(eng, db=db, pbk=pbk):
                        return eng.scalar_tensor_tensor(
                            xpre[:, 512 * db:512 * db + 512], pbk[:, :], 1.0,
                            du[:, 512 * db:512 * db + 512], ALU.mult, ALU.add,
                            accum_out=rs[:, db:db + 1])
                    sc.step("vector", ev, waits=[("s_t", None), ("s_s", None)],
                            incs=[("s_v", 1)] if db == 3 else [])
                emit_ln(sc, xpre, rs, 4, junk, sumsq, stat, [x1t[:]],
                        out_waits=[("s_o", None)])
                def st(eng, ch=ch):
                    return eng.dma_start(out=x1o[ch], in_=x1t[:])
                sc.step("sync", st, waits=[("s_s", None)], incs=[("s_o", 16)])


def build_stage1():
    return _solo_build(emit_stage1)


def emit_stage2(X):
    sc, nc = X.sc, X.nc
    x1 = X.di("x1", [CPC, 128, D], F32)
    wg = X.di("wg", [D, 8192], I8)
    wu = X.di("wu", [D, 8192], I8)
    wd = X.di("wd", [8192, D], I8)
    sg = X.di("sg", [64, 128], F32)
    su = X.di("su", [64, 128], F32)
    sd = X.di("sd", [1, D], F32)
    identB = X.di("identB", [128, 128], BF16)
    identF = X.di("identF", [128, 128], F32)
    onesr = X.di("onesr", [1, 128], F32)
    x2o = X.do("x2", [CPC, 128, D], F32)
    hT = X.ds("hT", [8192, TPC], BF16)
    if True:
        sb = X.sb
        s_identB = sb("s_identB", [128, 128], BF16)
        s_identF = sb("s_identF", [128, 128], F32)
        s_onesr = sb("s_onesr", [1, 128], F32)
        s_sg64 = sb("s_sg64", [64, 256], F32)
        s_sgc = sb("s_sgc", [128, 128], F32)
        s_sdrow = sb("s_sdrow", [1, D], F32)
        s_sdB = sb("s_sdB", [128, D], F32)
        xt_bf = sb("xt_bf", [128, 2 * D], BF16)
        xT = sb("xT", [128, 16 * TPC], BF16)
        wg_sb = sb("wg_sb", [128, 2 * 16 * 128], BF16)
        wu_sb = sb("wu_sb", [128, 2 * 16 * 128], BF16)
        gtmp = sb("gtmp", [128, TPC], BF16)
        h_bf = sb("h_bf", [128, TPC], BF16)
        hpair = sb("hpair", [128, 64 * 256], BF16)
        wd_sb = sb("wd_sb", [128, 2 * 2 * D], BF16)
        x1f = sb("x1f", [128, D], F32)
        xpre = sb("xpre", [128, D], F32)
        rs = sb("rs", [128, 4], F32)
        junk = sb("junk", [128, D], F32)
        sumsq = sb("sumsq", [128, 1], F32)
        stat = sb("stat", [128, 4], F32)
        x2t = sb("x2t", [128, D], F32)
        ps = [X.p512(i) for i in range(8)]

        def ld(dst, src):
            def f(eng, dst=dst, src=src):
                return eng.dma_start(out=dst, in_=src)
            return f

        sc.step("sync", ld(s_identB[:], identB), incs=[("s_w", 16)])
        sc.step("sync", ld(s_identF[:], identF), incs=[("s_w", 16)])
        sc.step("sync", ld(s_onesr[:], onesr), incs=[("s_w", 16)])
        sc.step("sync", ld(s_sg64[:, 0:128], sg), incs=[("s_w", 16)])
        sc.step("sync", ld(s_sg64[:, 128:256], su), incs=[("s_w", 16)])
        sc.step("sync", ld(s_sdrow[:], sd), incs=[("s_w", 16)])

        for k in range(2):
            def trs(eng, k=k):
                return eng.transpose(ps[6][0:128, 0:64].bitcast(F32),
                                     s_sg64[0:64, 128 * k:128 * k + 128],
                                     s_identF[0:64, 0:64])
            sc.step("tensor", trs, waits=[("s_w", None), ("s_v", None)],
                    incs=[("s_t", 1)])
            def evs(eng, k=k):
                return eng.tensor_copy(s_sgc[:, 64 * k:64 * k + 64],
                                       ps[6][0:128, 0:64].bitcast(F32))
            sc.step("vector", evs, waits=[("s_t", None)], incs=[("s_v", 1)])
        for blk in range(4):
            def bc(eng, blk=blk):
                return eng.matmul(ps[7][:, :], s_onesr[0:1, :],
                                  s_sdrow[0:1, 512 * blk:512 * blk + 512],
                                  start=True, stop=True)
            sc.step("tensor", bc, waits=[("s_w", None), ("s_v", None)],
                    incs=[("s_t", 1)])
            def ev(eng, blk=blk):
                return eng.tensor_copy(s_sdB[:, 512 * blk:512 * blk + 512],
                                       ps[7][:, :])
            sc.step("vector", ev, waits=[("s_t", None)], incs=[("s_v", 1)])

        for ci in range(CPC):
            cb = ci % 2
            def ldx(eng, ci=ci, cb=cb):
                return eng.dma_start(out=xt_bf[:, cb * D:(cb + 1) * D], in_=x1[ci])
            sc.step("gpsimd", ldx, waits=[("s_t", None)] if ci >= 2 else [],
                    incs=[("s_u", 16)])
            for dg in range(4):
                pb = ps[dg % 2]
                for s4 in range(4):
                    d_ = dg * 4 + s4
                    def tr(eng, ci=ci, cb=cb, d_=d_, s4=s4, pb=pb):
                        return eng.transpose(
                            pb[:].bitcast(BF16)[:, 128 * s4:128 * s4 + 128],
                            xt_bf[:, cb * D + d_ * 128:cb * D + d_ * 128 + 128],
                            s_identB[:])
                    sc.step("tensor", tr,
                            waits=[("s_u", None), ("s_w", None), ("s_v", None)]
                            if s4 == 0 else [],
                            incs=[("s_t", 1)] if s4 == 3 else [])
                def tev(eng, ci=ci, dg=dg, pb=pb):
                    src = pb[:].bitcast(BF16)[:, 0:512].rearrange(
                        "p (s j) -> p s j", j=128)
                    dst = xT[:].rearrange("p (d t) -> p d t", t=TPC)[
                        :, 4 * dg:4 * dg + 4, ci * 128:ci * 128 + 128]
                    return eng.tensor_copy(dst, src)
                sc.step("vector", tev, waits=[("s_t", None)], incs=[("s_v", 1)])

        s2_u0 = sc.count("s_u")
        s2_nb_t = [0, 0]
        for n_ in range(64):
            wb = n_ % 2
            def ldw(eng, n_=n_, wb=wb):
                return eng.dma_start(
                    out=wg_sb[:].rearrange("p (b c n) -> p b c n", b=2, n=128)[:, wb],
                    in_=wg.rearrange("(c p) n -> p c n", p=128)[
                        :, :, 128 * n_:128 * n_ + 128])
            w = [("s_t", s2_nb_t[n_])] if n_ >= 2 else []
            sc.step("gpsimd", ldw, waits=w, incs=[("s_u", 16)])
            def ldw2(eng, n_=n_, wb=wb):
                return eng.dma_start(
                    out=wu_sb[:].rearrange("p (b c n) -> p b c n", b=2, n=128)[:, wb],
                    in_=wu.rearrange("(c p) n -> p c n", p=128)[
                        :, :, 128 * n_:128 * n_ + 128])
            sc.step("gpsimd", ldw2, incs=[("s_u", 16)])

            if True:
                for dc in range(16):
                    for tb in range(4):
                        def mm(eng, n_=n_, wb=wb, dc=dc, tb=tb):
                            return eng.matmul(
                                ps[tb][:, :],
                                wg_sb[:, wb * 2048 + dc * 128:wb * 2048 + dc * 128 + 128],
                                xT[:, dc * TPC + tb * 512:dc * TPC + tb * 512 + 512],
                                start=(dc == 0), stop=(dc == 15))
                        w = []
                        if dc == 0 and tb == 0:
                            w = [("s_u", s2_u0 + 32 * (n_ + 1)),
                                 ("s_s", None), ("s_v", None)]
                        sc.step("tensor", mm, waits=w,
                                incs=[("s_t", 1)] if (dc == 15 and tb == 3) else [])
                for dc in range(16):
                    for tb in range(4):
                        def mm(eng, n_=n_, wb=wb, dc=dc, tb=tb):
                            return eng.matmul(
                                ps[4 + tb][:, :],
                                wu_sb[:, wb * 2048 + dc * 128:wb * 2048 + dc * 128 + 128],
                                xT[:, dc * TPC + tb * 512:dc * TPC + tb * 512 + 512],
                                start=(dc == 0), stop=(dc == 15))
                        sc.step("tensor", mm, waits=[],
                                incs=[("s_t", 1)] if (dc == 15 and tb == 3) else [])
                def sil(eng, n_=n_):
                    return eng.activation(gtmp[:], ps[0][:].ext(4),
                                          AF.Silu, scale=s_sgc[:, n_:n_ + 1])
                def sil2(eng, n_=n_):
                    return eng.activation(gtmp[:],
                                          ps[0].whole4()[:],
                                          AF.Silu, scale=s_sgc[:, n_:n_ + 1])
                for tb in range(4):
                    def silb(eng, n_=n_, tb=tb):
                        return eng.activation(gtmp[:, 512 * tb:512 * tb + 512],
                                              ps[tb][:, :], AF.Silu,
                                              scale=s_sgc[:, n_:n_ + 1])
                    sc.step("scalar", silb,
                            waits=[("s_t", sc.count("s_t") - 1), ("s_v", None)],
                            incs=[("s_s", 1)] if tb == 3 else [])
                for tb in range(4):
                    def hmul(eng, n_=n_, tb=tb):
                        return eng.scalar_tensor_tensor(
                            h_bf[:, 512 * tb:512 * tb + 512], ps[4 + tb][:, :],
                            s_sgc[:, 64 + n_:64 + n_ + 1],
                            gtmp[:, 512 * tb:512 * tb + 512],
                            ALU.mult, ALU.mult)
                    sc.step("vector", hmul,
                            waits=[("s_t", None), ("s_s", None), ("s_h", None)],
                            incs=[("s_v", 1)] if tb == 3 else [])
                def sth(eng, n_=n_):
                    return eng.dma_start(out=hT[128 * n_:128 * n_ + 128, :],
                                         in_=h_bf[:])
                sc.step("sync", sth, waits=[("s_v", None)], incs=[("s_h", 16)])
            s2_nb_t.append(sc.count("s_t"))

        s2_h_all = sc.count("s_h")
        s2_wd0 = sc.count("s_wd")
        s2_g16_t = {}
        for pt in range(8):
            for q4 in range(4):
                def ldh(eng, pt=pt, q4=q4):
                    return eng.dma_start(
                        out=hpair[:].rearrange("p (c t) -> p c t", t=256)[
                            :, 16 * q4:16 * q4 + 16],
                        in_=hT.rearrange("(c p) t -> p c t", p=128)[
                            :, 16 * q4:16 * q4 + 16, 256 * pt:256 * pt + 256])
                w = [("s_h", s2_h_all)]
                if pt >= 1:
                    w.append(("s_t", s2_g16_t[(pt - 1, q4)]))
                sc.step("sync", ldh, waits=w, incs=[("s_h", 16)])
            def mk_ldwd(pt, g2, wb):
                def ldwd(eng, pt=pt, g2=g2, wb=wb):
                    return eng.dma_start(
                        out=wd_sb[:].rearrange("p (b c j) -> p b c j", b=2, j=D)[:, wb],
                        in_=wd.rearrange("(c p) j -> p c j", p=128)[
                            :, 2 * g2:2 * g2 + 2, :])
                return ldwd
            for g2 in range(2):
                w = [("s_t", None)] if pt > 0 else []
                sc.step("gpsimd", mk_ldwd(pt, g2, g2 % 2), waits=w,
                        incs=[("s_wd", 16)])
            for g2 in range(32):
                wb = g2 % 2
                for k2 in range(2):
                    nc_ = g2 * 2 + k2
                    for tt in range(2):
                        for db in range(4):
                            def mm(eng, pt=pt, nc_=nc_, wb=wb, tt=tt, db=db):
                                return eng.matmul(
                                    ps[tt * 4 + db][:, :],
                                    hpair[:, nc_ * 256 + tt * 128:nc_ * 256 + tt * 128 + 128],
                                    wd_sb[:, wb * 2 * D + (nc_ % 2) * D + db * 512:
                                          wb * 2 * D + (nc_ % 2) * D + db * 512 + 512],
                                    start=(nc_ == 0), stop=(nc_ == 63))
                            w = []
                            if tt == 0 and db == 0:
                                w = [("s_wd", s2_wd0 + 16 * (pt * 32 + g2 + 1)),
                                     ("s_h", s2_h_all + 16 * (pt * 4 + g2 // 8 + 1))]
                                if nc_ == 0:
                                    w += [("s_s", None), ("s_v", None)]
                            sc.step("tensor", mm, waits=w,
                                    incs=[("s_t", 1)] if (k2 == 1 and tt == 1 and db == 3) else [])
                if g2 % 8 == 7:
                    s2_g16_t[(pt, g2 // 8)] = sc.count("s_t")
                if g2 + 2 < 32:
                    sc.step("gpsimd", mk_ldwd(pt, g2 + 2, (g2 + 2) % 2),
                            waits=[("s_t", None)], incs=[("s_wd", 16)])
            for tt in range(2):
                ci = pt * 2 + tt
                def ldx1(eng, ci=ci):
                    return eng.dma_start(out=x1f[:], in_=x1[ci])
                sc.step("sync", ldx1, waits=[("s_s", None), ("s_v", None)],
                        incs=[("s_r", 16)])
                for db in range(4):
                    def ev1(eng, tt=tt, db=db):
                        return eng.tensor_mul(xpre[:, 512 * db:512 * db + 512],
                                              ps[tt * 4 + db][:, :],
                                              s_sdB[:, 512 * db:512 * db + 512])
                    sc.step("vector", ev1,
                            waits=[("s_t", None), ("s_r", None), ("s_s", None)])
                    def ev2(eng, tt=tt, db=db):
                        return eng.scalar_tensor_tensor(
                            xpre[:, 512 * db:512 * db + 512],
                            xpre[:, 512 * db:512 * db + 512], 1.0,
                            x1f[:, 512 * db:512 * db + 512],
                            ALU.mult, ALU.add, accum_out=rs[:, db:db + 1])
                    sc.step("vector", ev2, waits=[],
                            incs=[("s_v", 1)] if db == 3 else [])
                emit_ln(sc, xpre, rs, 4, junk, sumsq, stat, [x2t[:]],
                        out_waits=[("s_o", None)])
                def stx(eng, ci=ci):
                    return eng.dma_start(out=x2o[ci], in_=x2t[:])
                sc.step("sync", stx, waits=[("s_s", None)], incs=[("s_o", 16)])


def build_stage2():
    return _solo_build(emit_stage2)


def emit_stage3(X, cc_after_kv=None):
    sc, nc = X.sc, X.nc
    x2 = X.di("x2", [CPC, 128, D], F32)
    wqkv = X.di("wqkv", [D, 3 * D], F32)
    lnsc = X.di("lnsc", [128, 1], F32)
    lnbi = X.di("lnbi", [128, 1], F32)
    identB = X.di("identB", [128, 128], BF16)
    qT_o = X.do("qT_l", [D, TPC], BF16)
    kv = X.do("kv", [2 * D, TPC], BF16)
    kT_o = kv[0:D]
    cosl = X.di("cosl", [CPC, 128, 64], F32)
    sinl = X.di("sinl", [CPC, 128, 64], F32)
    if True:
        sb = X.sb
        s_identB = sb("s_identB", [128, 128], BF16)
        s_ln = sb("s_ln", [128, 2], F32)
        s_cos = sb("s_cos", [128, CPC * 64], F32)
        s_sin = sb("s_sin", [128, CPC * 64], F32)
        xt_bf = sb("xt_bf", [128, 2 * D], BF16)
        xT = sb("xT", [128, 16 * TPC], BF16)
        w_sb = sb("w_sb", [128, 2 * 16 * 512], BF16)
        qfw = sb("qfw", [128, 8192], BF16)
        qrw = sb("qrw", [128, 8192], F32)
        qsqw = sb("qsqw", [128, 8192], BF16)
        zqw = sb("zqw", [128, 8192], BF16)
        ssw = sb("ssw", [128, 7 * 64], F32)
        qts = sb("qts", [128, 512], BF16)
        vtile = sb("vtile", [128, 512], BF16)
        ps = [X.p512(i) for i in range(4)]

        def ld(dst, src):
            def f(eng, dst=dst, src=src):
                return eng.dma_start(out=dst, in_=src)
            return f

        sc.step("sync", ld(s_identB[:], identB), incs=[("s_w", 16)])
        sc.step("sync", ld(s_ln[:, 0:1], lnsc), incs=[("s_w", 16)])
        sc.step("sync", ld(s_ln[:, 1:2], lnbi), incs=[("s_w", 16)])
        sc.step("sync", ld(s_cos[:].rearrange("p (c j) -> p c j", j=64),
                           cosl.rearrange("c p j -> p c j")), incs=[("s_w", 16)])
        sc.step("sync", ld(s_sin[:].rearrange("p (c j) -> p c j", j=64),
                           sinl.rearrange("c p j -> p c j")), incs=[("s_w", 16)])

        s3_u0 = sc.count("s_u")
        for ci in range(CPC):
            cb = ci % 2
            def ldx(eng, ci=ci, cb=cb):
                return eng.dma_start(out=xt_bf[:, cb * D:(cb + 1) * D], in_=x2[ci])
            sc.step("gpsimd", ldx, waits=[("s_t", None)] if ci >= 2 else [],
                    incs=[("s_u", 16)])
            for dg in range(4):
                pb = ps[dg % 2]
                for s4 in range(4):
                    d_ = dg * 4 + s4
                    def tr(eng, ci=ci, cb=cb, d_=d_, s4=s4, pb=pb):
                        return eng.transpose(
                            pb[:].bitcast(BF16)[:, 128 * s4:128 * s4 + 128],
                            xt_bf[:, cb * D + d_ * 128:cb * D + d_ * 128 + 128],
                            s_identB[:])
                    sc.step("tensor", tr,
                            waits=[("s_u", None), ("s_w", None), ("s_v", None)]
                            if s4 == 0 else [],
                            incs=[("s_t", 1)] if s4 == 3 else [])
                def tev(eng, ci=ci, dg=dg, pb=pb):
                    src = pb[:].bitcast(BF16)[:, 0:512].rearrange(
                        "p (s j) -> p s j", j=128)
                    dst = xT[:].rearrange("p (d t) -> p d t", t=TPC)[
                        :, 4 * dg:4 * dg + 4, ci * 128:ci * 128 + 128]
                    return eng.tensor_copy(dst, src)
                sc.step("vector", tev, waits=[("s_t", None)], incs=[("s_v", 1)])

        s3_nsb_t = []
        bank_free = [0, 0]
        pending = None
        order = [4, 5, 6, 7, 8, 9, 10, 11, 0, 1, 2, 3]
        for pos, nsb in enumerate(order):
            wb = pos % 2
            def ldw(eng, nsb=nsb, wb=wb):
                return eng.dma_start(
                    out=w_sb[:].rearrange("p (b c n) -> p b c n", b=2, n=512)[:, wb],
                    in_=wqkv.rearrange("(c p) n -> p c n", p=128)[
                        :, :, 512 * nsb:512 * nsb + 512])
            sc.step("gpsimd", ldw,
                    waits=[("s_t", s3_nsb_t[pos - 2])] if pos >= 2 else [],
                    incs=[("s_u", 16)])
            for ci in range(CPC):
                pb = ps[2 + (ci % 2)]
                for dc in range(16):
                    def mm(eng, nsb=nsb, wb=wb, ci=ci, dc=dc, pb=pb):
                        return eng.matmul(
                            pb[:, :],
                            xT[:, dc * TPC + ci * 128:dc * TPC + ci * 128 + 128],
                            w_sb[:, wb * 16 * 512 + dc * 512:wb * 16 * 512 + dc * 512 + 512],
                            start=(dc == 0), stop=(dc == 15))
                    w = []
                    if dc == 0:
                        w = [("s_u", s3_u0 + 16 * (CPC + pos + 1)),
                             ("s_v", bank_free[ci % 2])]
                    sc.step("tensor", mm, waits=w,
                            incs=[("s_t", 1)] if dc == 15 else [])
                if nsb < 8:
                    def evq(eng, ci=ci, pb=pb):
                        return eng.tensor_copy(qfw[:, ci * 512:ci * 512 + 512],
                                               pb[:, :])
                    sc.step("vector", evq, waits=[("s_t", None)],
                            incs=[("s_v", 1)])
                else:
                    def vev(eng, pb=pb):
                        return eng.tensor_copy(vtile[:], pb[:, :])
                    sc.step("vector", vev, waits=[("s_t", None), ("s_o", None)],
                            incs=[("s_v", 1)])
                    def vst(eng, nsb=nsb, ci=ci):
                        return eng.dma_start(
                            out=kv[D + ci * 128:D + ci * 128 + 128,
                                   512 * (nsb - 8):512 * (nsb - 8) + 512],
                            in_=vtile[:])
                    sc.step("sync", vst, waits=[("s_v", None)], incs=[("s_o", 16)])
                bank_free[ci % 2] = sc.count("s_v")
            if pending is not None:
                _emit_qk_store(sc, pending, zqw, s_ln, s_identB, ps, qts,
                               qT_o if pending < 4 else kT_o)
                pending = None
            if nsb < 8:
                _emit_qk_wide(sc, nsb, qfw, qrw, qsqw, zqw, ssw, s_cos, s_sin,
                              s_ln, ps)
                pending = nsb
            s3_nsb_t.append(sc.count("s_t"))
            if pos == 7 and cc_after_kv is not None:
                cc_after_kv()
        if pending is not None:
            _emit_qk_store(sc, pending, zqw, s_ln, s_identB, ps, qts,
                           qT_o if pending < 4 else kT_o)


def _emit_qk_wide(sc, nsb, qfw, qrw, qsqw, zqw, ssw, s_cos, s_sin, s_ln, ps):
    def v4(t, half, dtl=128):
        return t[:].rearrange("p (c h j) -> p c h j", h=4, j=128)[
            :, :, :, half * 64:half * 64 + 64]
    def cs4(t):
        return t[:].rearrange("p (c o j) -> p c o j", o=1, j=64).broadcast_to(
            (128, CPC, 4, 64))
    def r1(eng):
        return eng.tensor_tensor(v4(qrw, 0), v4(qfw, 0), cs4(s_cos), ALU.mult)
    sc.step("vector", r1, waits=[])
    def r2(eng):
        return eng.tensor_tensor(v4(qrw, 1), v4(qfw, 1), cs4(s_cos), ALU.mult)
    sc.step("vector", r2, waits=[])
    def r3(eng):
        return eng.tensor_tensor(v4(qsqw, 0), v4(qfw, 1), cs4(s_sin), ALU.mult)
    sc.step("vector", r3, waits=[])
    def r4(eng):
        return eng.tensor_tensor(v4(qsqw, 1), v4(qfw, 0), cs4(s_sin), ALU.mult)
    sc.step("vector", r4, waits=[])
    def r5(eng):
        return eng.tensor_sub(v4(qrw, 0), v4(qrw, 0), v4(qsqw, 0))
    sc.step("vector", r5, waits=[])
    def r6(eng):
        return eng.tensor_add(v4(qrw, 1), v4(qrw, 1), v4(qsqw, 1))
    sc.step("vector", r6, waits=[], incs=[("s_v", 1)])
    def sq(eng):
        return eng.activation(qsqw[:], qrw[:], AF.Square)
    sc.step("scalar", sq, waits=[("s_v", None)], incs=[("s_s", 1)])
    def seg(t):
        return t[:].rearrange("p (s j) -> p s j", j=128)
    def rd1(eng):
        return eng.tensor_reduce(ssw[:, 0:64], seg(qrw), AXX, ALU.add)
    sc.step("vector", rd1, waits=[("s_s", None)])
    def rd2(eng):
        return eng.tensor_reduce(ssw[:, 64:128], seg(qsqw), AXX, ALU.add)
    sc.step("vector", rd2, waits=[])
    def st1(eng):
        return eng.tensor_scalar_mul(ssw[:, 128:192], ssw[:, 0:64], 1.0 / 128)
    sc.step("vector", st1, waits=[])
    def st2(eng):
        return eng.tensor_scalar_mul(ssw[:, 192:256], ssw[:, 64:128], 1.0 / 128)
    sc.step("vector", st2, waits=[])
    def st3(eng):
        return eng.tensor_mul(ssw[:, 256:320], ssw[:, 128:192], ssw[:, 128:192])
    sc.step("vector", st3, waits=[])
    def st4(eng):
        return eng.tensor_sub(ssw[:, 192:256], ssw[:, 192:256], ssw[:, 256:320])
    sc.step("vector", st4, waits=[])
    def st5(eng):
        return eng.tensor_scalar_add(ssw[:, 192:256], ssw[:, 192:256], float(EPS))
    sc.step("vector", st5, waits=[], incs=[("s_v", 1)])
    def st6(eng):
        return eng.activation(ssw[:, 320:384], ssw[:, 192:256], AF.Sqrt)
    sc.step("scalar", st6, waits=[("s_v", None)], incs=[("s_s", 1)])
    def st7(eng):
        return eng.reciprocal(ssw[:, 320:384], ssw[:, 320:384])
    sc.step("vector", st7, waits=[("s_s", None)])
    def st8(eng):
        return eng.tensor_scalar_mul(ssw[:, 256:320], ssw[:, 128:192], -1.0)
    sc.step("vector", st8, waits=[])
    def bc(col):
        return ssw[:, col:col + 64].rearrange("p (s o) -> p s o", o=1).broadcast_to(
            (128, 64, 128))
    def z1(eng):
        return eng.tensor_tensor(seg(qrw), seg(qrw), bc(256), ALU.add)
    sc.step("vector", z1, waits=[])
    def z2(eng):
        return eng.tensor_tensor(seg(zqw), seg(qrw), bc(320), ALU.mult)
    sc.step("vector", z2, waits=[("s_t", None)], incs=[("s_v", 1)])


def _emit_qk_store(sc, nsb, zqw, s_ln, s_identB, ps, qts, dst_dram):
    for ci in range(CPC):
        pt = ps[ci % 2]
        for h in range(4):
            def tr(eng, ci=ci, h=h, pt=pt):
                return eng.transpose(pt[:].bitcast(BF16)[:, 128 * h:128 * h + 128],
                                     zqw[:, ci * 512 + h * 128:ci * 512 + h * 128 + 128],
                                     s_identB[:])
            sc.step("tensor", tr, waits=[("s_v", None)] if h == 0 else [],
                    incs=[("s_t", 1)] if h == 3 else [])
        def gb(eng, pt=pt):
            return eng.tensor_scalar(qts[:], pt[:].bitcast(BF16)[:, 0:512],
                                     s_ln[:, 0:1], s_ln[:, 1:2], ALU.mult, ALU.add)
        sc.step("vector", gb, waits=[("s_t", None), ("s_o", None)],
                incs=[("s_v", 1)])
        h0 = (nsb % 4) * 4
        def stq(eng, h0=h0, ci=ci, dst_dram=dst_dram):
            dst = dst_dram.rearrange("(h dh) t -> dh h t", dh=128)[
                :, h0:h0 + 4, ci * 128:ci * 128 + 128]
            src = qts[:].rearrange("p (h t) -> p h t", t=128)
            return eng.dma_start(out=dst, in_=src)
        sc.step("sync", stq, waits=[("s_v", None)], incs=[("s_o", 16)])


def build_stage3():
    return _solo_build(emit_stage3)


def _emit_qk_tile(sc, nsb, ci, pb, qf, qr, qsq, zq, qts, sst, s_cos, s_sin,
                  s_ln, s_identB, ps, dst_dram):
    h0 = (nsb % 4) * 4

    def evq(eng):
        return eng.tensor_copy(qf[:], pb[:, :])
    sc.step("vector", evq, waits=[("s_t", None)], incs=[("s_v", 1)])

    def ap_half(t, off):
        return t[:].rearrange("p (h j) -> p h j", j=128)[:, :, off:off + 64]
    c_ap = s_cos[:].rearrange("p (c j) -> p c j", j=64)[:, ci]
    s_ap = s_sin[:].rearrange("p (c j) -> p c j", j=64)[:, ci]
    def rope1(eng):
        return eng.tensor_tensor(ap_half(qr, 0), ap_half(qf, 0),
                                 _bcast4(s_cos, ci), ALU.mult)
    sc.step("vector", rope1, waits=[])
    def rope2(eng):
        return eng.tensor_tensor(ap_half(qsq, 0), ap_half(qf, 64),
                                 _bcast4(s_sin, ci), ALU.mult)
    sc.step("vector", rope2, waits=[])
    def rope3(eng):
        return eng.tensor_sub(ap_half(qr, 0), ap_half(qr, 0), ap_half(qsq, 0))
    sc.step("vector", rope3, waits=[])
    def rope4(eng):
        return eng.tensor_tensor(ap_half(qr, 64), ap_half(qf, 0),
                                 _bcast4(s_sin, ci), ALU.mult)
    sc.step("vector", rope4, waits=[])
    def rope5(eng):
        return eng.tensor_tensor(ap_half(qsq, 64), ap_half(qf, 64),
                                 _bcast4(s_cos, ci), ALU.mult)
    sc.step("vector", rope5, waits=[])
    def rope6(eng):
        return eng.tensor_add(ap_half(qr, 64), ap_half(qr, 64), ap_half(qsq, 64))
    sc.step("vector", rope6, waits=[], incs=[("s_v", 1)])

    def sq(eng):
        return eng.activation(qsq[:], qr[:], AF.Square)
    sc.step("scalar", sq, waits=[("s_v", None)], incs=[("s_s", 1)])
    for h in range(4):
        def r1(eng, h=h):
            return eng.reduce_sum(sst[:, h:h + 1], qr[:, h * 128:h * 128 + 128],
                                  AXX)
        sc.step("vector", r1, waits=[("s_s", None)])
        def r2(eng, h=h):
            return eng.reduce_sum(sst[:, 4 + h:5 + h], qsq[:, h * 128:h * 128 + 128],
                                  AXX)
        sc.step("vector", r2, waits=[])
    def st1(eng):
        return eng.tensor_scalar_mul(sst[:, 8:12], sst[:, 0:4], 1.0 / 128)
    sc.step("vector", st1, waits=[])
    def st2(eng):
        return eng.tensor_scalar_mul(sst[:, 12:16], sst[:, 4:8], 1.0 / 128)
    sc.step("vector", st2, waits=[])
    def st3(eng):
        return eng.tensor_mul(sst[:, 16:20], sst[:, 8:12], sst[:, 8:12])
    sc.step("vector", st3, waits=[])
    def st4(eng):
        return eng.tensor_sub(sst[:, 12:16], sst[:, 12:16], sst[:, 16:20])
    sc.step("vector", st4, waits=[])
    def st5(eng):
        return eng.tensor_scalar_add(sst[:, 12:16], sst[:, 12:16], float(EPS))
    sc.step("vector", st5, waits=[], incs=[("s_v", 1)])
    def st6(eng):
        return eng.activation(sst[:, 16:20], sst[:, 12:16], AF.Sqrt)
    sc.step("scalar", st6, waits=[("s_v", None)], incs=[("s_s", 1)])
    def st7(eng):
        return eng.reciprocal(sst[:, 16:20], sst[:, 16:20])
    sc.step("vector", st7, waits=[("s_s", None)])
    def st8a(eng):
        return eng.tensor_mul(sst[:, 20:24], sst[:, 8:12], sst[:, 16:20])
    sc.step("vector", st8a, waits=[])
    def st8b(eng):
        return eng.tensor_scalar_mul(sst[:, 20:24], sst[:, 20:24], -1.0)
    sc.step("vector", st8b, waits=[], incs=[("s_v", 1)])
    for h in range(4):
        def nor(eng, h=h):
            return eng.activation(zq[:, h * 128:h * 128 + 128],
                                  qr[:, h * 128:h * 128 + 128], AF.Identity,
                                  scale=sst[:, 16 + h:17 + h],
                                  bias=sst[:, 20 + h:21 + h])
        sc.step("scalar", nor, waits=[("s_v", None)], incs=[("s_s", 1)] if h == 3 else [])
    pt = ps[0] if (ci % 2) == 0 else ps[1]
    for h in range(4):
        def tr(eng, h=h, pt=pt):
            return eng.transpose(pt[:].bitcast(BF16)[:, 128 * h:128 * h + 128],
                                 zq[:, h * 128:h * 128 + 128], s_identB[:])
        sc.step("tensor", tr, waits=[("s_s", None), ("s_v", None)] if h == 0 else [],
                incs=[("s_t", 1)] if h == 3 else [])
    def gb(eng, pt=pt):
        return eng.tensor_scalar(qts[:], pt[:].bitcast(BF16)[:, 0:512],
                                 s_ln[:, 0:1], s_ln[:, 1:2], ALU.mult, ALU.add)
    sc.step("vector", gb, waits=[("s_t", None), ("s_o", None)], incs=[("s_v", 1)])
    def stq(eng, h0=h0, ci=ci, dst_dram=dst_dram):
        dst = dst_dram.rearrange("(h dh) t -> dh h t", dh=128)[
            :, h0:h0 + 4, ci * 128:ci * 128 + 128]
        src = qts[:].rearrange("p (h t) -> p h t", t=128)
        return eng.dma_start(out=dst, in_=src)
    sc.step("sync", stq, waits=[("s_v", None)], incs=[("s_o", 16)])


def _bcast4(t, ci):
    sl = t[:, ci * 64:ci * 64 + 64]
    return sl.rearrange("p (o j) -> p o j", o=1).broadcast_to((128, 4, 64))


def _smin_u(M):
    return max(0, -(-(M - 7) // 8))


def emit_stage4(X, fused_kv=False):
    sc, nc = X.sc, X.nc
    qT_l = X.di("qT_l", [D, TPC], BF16)
    if fused_kv:
        kv_ga = X.ds("kv_ga", [NCORES * 2 * D, TPC], BF16)
        kT_a = v_a = None
    else:
        kT_a = X.di("kT_a", [B, 16, 128, SEQ], BF16)
        v_a = X.di("v_a", [B, 16, SEQ, 128], BF16)
    masks = X.di("masks", [64, 128, 128], BF16)
    onesc = X.di("onesc", [128, 1], BF16)
    onesr = X.di("onesr", [1, 128], F32)
    aoT_o = X.do("aoT", [D, TPC], BF16)
    if True:
        sb = X.sb
        s_onesc = sb("s_onesc", [128, 1], BF16)
        s_onesr = sb("s_onesr", [1, 128], F32)
        s_masks = sb("s_masks", [128, 64 * 128], BF16)
        q_sb = sb("q_sb", [128, 2 * 1024], BF16)
        k_sb = sb("k_sb", [128, 2 * SEQ], BF16)
        v_sb = sb("v_sb", [128, 2 * SEQ], BF16)
        a2 = sb("a2", [128, 2 * 1024], BF16)
        a4 = sb("a4", [128, 4 * 1024], BF16)
        den = sb("den", [1, 1024], F32)
        rb = sb("rb", [128, 1024], F32)
        ao = sb("ao", [128, 1024], BF16)
        pa = [X.p1024(0), X.p1024(1)]
        po = X.p1024(2)
        pd = [X.pd512(0), X.pd512(1)]

        def ld(dst, src):
            def f(eng, dst=dst, src=src):
                return eng.dma_start(out=dst, in_=src)
            return f
        sc.step("sync", ld(s_onesc[:], onesc), incs=[("s_w", 16)])
        sc.step("sync", ld(s_onesr[:], onesr), incs=[("s_w", 16)])
        sc.step("sync", ld(s_masks[:].rearrange("p (m j) -> p m j", j=128),
                           masks.rearrange("m p j -> p m j")), incs=[("s_w", 16)])

        bh_t_end = []
        s4_kv0 = sc.count("s_kv")
        for bh in range(32):
            b, h = bh // 16, bh % 16
            kb = bh % 2
            def ldq(eng, b=b, h=h, kb=kb):
                return eng.dma_start(
                    out=q_sb[:, kb * 1024:(kb + 1) * 1024],
                    in_=qT_l[h * 128:(h + 1) * 128, b * 1024:(b + 1) * 1024])
            w = [("s_t", bh_t_end[bh - 2])] if bh >= 2 else []
            if fused_kv:
                w = w + [("s_cc", 1)]
            sc.step("sync", ldq, waits=w, incs=[("s_kv", 16)])
            if not fused_kv:
                def ldk(eng, b=b, h=h, kb=kb):
                    return eng.dma_start(out=k_sb[:, kb * SEQ:(kb + 1) * SEQ],
                                         in_=kT_a[b, h])
                def ldv(eng, b=b, h=h, kb=kb):
                    return eng.dma_start(
                        out=v_sb[:, kb * SEQ:(kb + 1) * SEQ].rearrange(
                            "p (c j) -> p c j", j=128),
                        in_=v_a[b, h].rearrange("(c p) j -> p c j", p=128))
                sc.step("sync", ldk, incs=[("s_kv", 16)])
                sc.step("sync", ldv, incs=[("s_kv", 16)])
                nload = 3
            else:
                kvr = kv_ga.rearrange("(r x) t -> r x t", r=NCORES)
                for rank in range(NCORES):
                    def ldk(eng, b=b, h=h, kb=kb, rank=rank, kvr=kvr):
                        src = kvr[rank, h * 128:(h + 1) * 128].rearrange(
                            "dh (bb lc pp) -> dh bb lc pp", bb=2, pp=128)[:, b]
                        dst = k_sb[:, kb * SEQ:(kb + 1) * SEQ].rearrange(
                            "p (lc rk j) -> p rk lc j", rk=8, j=128)[:, rank]
                        return eng.dma_start(out=dst, in_=src)
                    sc.step("sync", ldk, incs=[("s_kv", 16)])
                for rank in range(NCORES):
                    def ldv(eng, b=b, h=h, kb=kb, rank=rank, kvr=kvr):
                        src0 = kvr[rank, D:2 * D].rearrange(
                            "(bb lc pp) t -> pp bb lc t", bb=2, pp=128)[:, b]
                        src = src0[:, :, h * 128:(h + 1) * 128]
                        dst = v_sb[:, kb * SEQ:(kb + 1) * SEQ].rearrange(
                            "p (lc rk j) -> p rk lc j", rk=8, j=128)[:, rank]
                        return eng.dma_start(out=dst, in_=src)
                    sc.step("sync", ldv, incs=[("s_kv", 16)])
                nload = 17
            kvbase = s4_kv0 + 16 * nload * (bh + 1)

            p1_ss = []
            v_done = []

            def emit_av2(Mav):
                c0a = _smin_u(Mav) * 128
                mb4 = (Mav % 4) * 1024
                vsl = slice(kb * SEQ + Mav * 128, kb * SEQ + Mav * 128 + 128)
                mms = []
                if c0a < 512:
                    def mo1(eng, Mav=Mav, c0a=c0a, mb4=mb4, vsl=vsl):
                        return eng.matmul(po[:, c0a:512], v_sb[:, vsl],
                                          a4[:, mb4 + c0a:mb4 + 512],
                                          start=(Mav == 0), stop=(Mav == 31))
                    def md1(eng, Mav=Mav, c0a=c0a, mb4=mb4):
                        return eng.matmul(pd[0][:, c0a:512], s_onesc[:],
                                          a4[:, mb4 + c0a:mb4 + 512],
                                          start=(Mav == 0), stop=(Mav == 31))
                    mms += [mo1, md1]
                lo2 = max(512, c0a)
                def mo2(eng, Mav=Mav, lo2=lo2, mb4=mb4, vsl=vsl):
                    return eng.matmul(po[:, lo2:1024], v_sb[:, vsl],
                                      a4[:, mb4 + lo2:mb4 + 1024],
                                      start=(Mav == 0), stop=(Mav == 63))
                def md2(eng, Mav=Mav, lo2=lo2, mb4=mb4):
                    return eng.matmul(pd[1][:, lo2 - 512:512], s_onesc[:],
                                      a4[:, mb4 + lo2:mb4 + 1024],
                                      start=(Mav == 0), stop=(Mav == 63))
                mms += [mo2, md2]
                for i, fn in enumerate(mms):
                    w = [("s_v", v_done[Mav])] if i == 0 else []
                    sc.step("tensor", fn, waits=w,
                            incs=[("s_t", 1)] if i == len(mms) - 1 else [])

            for M in range(64):
                c0 = _smin_u(M) * 128
                ap_ = pa[M % 2]
                mb = (M % 2) * 1024
                mb4 = (M % 4) * 1024
                def mm_a1(eng, M=M, c0=c0, ap_=ap_, kb=kb):
                    return eng.matmul(
                        ap_[:, c0:512], k_sb[:, kb * SEQ + M * 128:kb * SEQ + M * 128 + 128],
                        q_sb[:, kb * 1024 + c0:kb * 1024 + 512],
                        start=True, stop=True)
                def mm_a2(eng, M=M, c0=c0, ap_=ap_, kb=kb):
                    cc = max(c0, 512)
                    return eng.matmul(
                        ap_[:, cc:1024], k_sb[:, kb * SEQ + M * 128:kb * SEQ + M * 128 + 128],
                        q_sb[:, kb * 1024 + cc:kb * 1024 + 1024],
                        start=True, stop=True)
                w = []
                if M == 0:
                    w = [("s_kv", kvbase), ("s_w", None), ("s_v", None),
                         ("s_s", None)]
                elif M >= 2:
                    w = [("s_s", p1_ss[M - 2])]
                first = True
                if c0 < 512:
                    sc.step("tensor", mm_a1, waits=w, incs=[])
                    first = False
                sc.step("tensor", mm_a2, waits=w if first else [],
                        incs=[("s_t", 1)])
                if M >= 2:
                    emit_av2(M - 2)
                def p1(eng, M=M, c0=c0, ap_=ap_, mb=mb):
                    return eng.activation(a2[:, mb + c0:mb + 1024],
                                          ap_[:, c0:1024], AF.Square)
                wp1 = [("s_t", None)]
                if M >= 2:
                    wp1.append(("s_v", v_done[M - 2]))
                sc.step("scalar", p1, waits=wp1, incs=[("s_s", 1)])
                p1_ss.append(sc.count("s_s"))
                def vm1(eng, M=M, c0=c0, mb=mb):
                    sl = slice(mb + c0, mb + c0 + 128)
                    return eng.tensor_mul(a2[:, sl], a2[:, sl],
                                          s_masks[:, M * 128:M * 128 + 128])
                sc.step("vector", vm1, waits=[("s_s", None)])
                def vm2(eng, M=M, c0=c0, mb=mb, mb4=mb4):
                    return eng.tensor_mul(a4[:, mb4 + c0:mb4 + 1024],
                                          a2[:, mb + c0:mb + 1024],
                                          a2[:, mb + c0:mb + 1024])
                sc.step("vector", vm2, waits=[], incs=[("s_v", 1)])
                v_done.append(sc.count("s_v"))
            emit_av2(62)
            emit_av2(63)

            def dev(eng):
                eng.tensor_copy(den[:, 0:512], pd[0][:, :])
                return eng.tensor_copy(den[:, 512:1024], pd[1][:, :])
            sc.step("vector", dev, waits=[("s_t", None)], incs=[("s_v", 1)])
            def drc(eng):
                return eng.reciprocal(den[:], den[:])
            sc.step("vector", drc, waits=[])
            def dcast(eng):
                return eng.tensor_copy(rb[0:1, :], den[:])
            sc.step("vector", dcast, waits=[], incs=[("s_v", 1)])
            for half in range(2):
                def bc(eng, half=half):
                    return eng.matmul(pa[0][:, half * 512:half * 512 + 512],
                                      s_onesr[0:1, :],
                                      rb[0:1, half * 512:half * 512 + 512],
                                      start=True, stop=True)
                sc.step("tensor", bc, waits=[("s_v", None), ("s_s", None)],
                        incs=[("s_t", 1)] if half == 1 else [])
            def rbe(eng):
                return eng.tensor_copy(rb[:], pa[0][:, :])
            sc.step("vector", rbe, waits=[("s_t", None)], incs=[("s_v", 1)])
            def fmul(eng):
                return eng.tensor_mul(ao[:], po[:, :], rb[:])
            sc.step("vector", fmul, waits=[("s_o", None)], incs=[("s_v", 1)])
            def sto(eng, b=b, h=h):
                return eng.dma_start(
                    out=aoT_o[h * 128:(h + 1) * 128, b * 1024:(b + 1) * 1024],
                    in_=ao[:])
            sc.step("sync", sto, waits=[("s_v", None)], incs=[("s_o", 16)])
            bh_t_end.append(sc.count("s_t"))


def build_stage4():
    return _solo_build(emit_stage4, False)


def _emit_av(sc, M, kb, a4, v_sb, s_onesc, po, pd):
    c0 = _smin_u(M) * 128
    mb = (M % 2) * 1024
    vsl = slice(kb * SEQ + M * 128, kb * SEQ + M * 128 + 128)
    m1lo, m1hi = c0 + 128, 512
    m2lo, m2hi = max(512, c0 + 128), 1024

    steps = []
    if m1lo < m1hi:
        def mo1(eng, M=M, m1lo=m1lo, m1hi=m1hi, vsl=vsl, mb=mb):
            return eng.matmul(po[:, m1lo:m1hi], v_sb[:, vsl],
                              a4[:, mb + m1lo:mb + m1hi],
                              start=(M == 0), stop=(M == 24))
        def md1(eng, M=M, m1lo=m1lo, m1hi=m1hi, mb=mb):
            return eng.matmul(pd[0][:, m1lo:m1hi], s_onesc[:],
                              a4[:, mb + m1lo:mb + m1hi],
                              start=(M == 0), stop=(M == 24))
        steps += [(mo1, [("s_s", None)]), (md1, [])]
    if m2lo < m2hi:
        def mo2(eng, M=M, m2lo=m2lo, m2hi=m2hi, vsl=vsl, mb=mb):
            return eng.matmul(po[:, m2lo:m2hi], v_sb[:, vsl],
                              a4[:, mb + m2lo:mb + m2hi],
                              start=(M == 0), stop=(M == 62))
        def md2(eng, M=M, m2lo=m2lo, m2hi=m2hi, mb=mb):
            return eng.matmul(pd[1][:, m2lo - 512:m2hi - 512], s_onesc[:],
                              a4[:, mb + m2lo:mb + m2hi],
                              start=(M == 0), stop=(M == 62))
        steps += [(mo2, [("s_s", None)] if not steps else []), (md2, [])]
    def mob(eng, M=M, c0=c0, vsl=vsl, mb=mb):
        return eng.matmul(po[:, c0:c0 + 128], v_sb[:, vsl],
                          a4[:, mb + c0:mb + c0 + 128],
                          start=False, stop=(M == 63))
    def mdb(eng, M=M, c0=c0, mb=mb):
        pdx = pd[0] if c0 < 512 else pd[1]
        off = 0 if c0 < 512 else 512
        return eng.matmul(pdx[:, c0 - off:c0 - off + 128], s_onesc[:],
                          a4[:, mb + c0:mb + c0 + 128],
                          start=False, stop=(M == 63))
    steps += [(mob, [("s_v", None)]), (mdb, [])]
    if M == 0 and steps:
        steps[0] = (steps[0][0], list(steps[0][1]) + [("s_v", None)])
    for i, (fn, w) in enumerate(steps):
        sc.step("tensor", fn, waits=w,
                incs=[("s_t", 1)] if i == len(steps) - 1 else [])


def build_fused():
    nc = bass.Bass("TRN2", target_bir_lowering=False)
    sc = Script()
    with contextlib.ExitStack() as st_:
        E = st_.enter_context
        arena = E(nc.sbuf_tensor("arena", [128, _ARENA_BYTES], I8))
        psA = E(nc.psum_tensor("psA", [128, 2048], F32))
        psB = E(nc.psum_tensor("psB", [128, 2048], F32))
        sems = {n: E(nc.semaphore(n)) for n in _ALL_SEMS}
        block = E(nc.Block())
        X = Ctx(nc, sc, sems, arena[:], psA[:], psB[:], "fused")
        nst = int(os.environ.get("FSTAGES", "6"))

        emit_stage1(X)
        if nst >= 2:
            X.barrier(); X.reset_arena()
            emit_stage2(X)
        if nst >= 3:
            X.barrier(); X.reset_arena()
            def cc_after_kv():
                kv = X.dram["kv"]
                kv_ga = X.ds("kv_ga", [NCORES * 2 * D, TPC], BF16)
                def cc(eng):
                    return eng.collective_compute(
                        "AllGather", ALU.bypass,
                        replica_groups=[list(range(NCORES))],
                        ins=[kv], outs=[kv_ga])
                sc.step("gpsimd", cc, waits=[("s_o", sc.count("s_o"))],
                        incs=[("s_cc", 1)])
            emit_stage3(X, cc_after_kv if nst >= 4 else None)
        if nst >= 4:
            X.barrier(); X.reset_arena()
            emit_stage4(X, fused_kv=True)
        if nst >= 5:
            X.barrier(); X.reset_arena()
            emit_stage5(X)
        if nst >= 6:
            X.barrier(); X.reset_arena()
            emit_stage6(X)

        sc.step("sync", lambda eng: None, waits=[("s_o", sc.count("s_o"))])
        emit_script(sc, block, sems)
    return nc


def _run_fused(inputs):
    if "fused" not in _PROGS:
        _PROGS["fused"] = build_fused()
    C = _consts()
    identF = np.eye(128, dtype=np.float32)
    x0_shards = _shard_tokens(np.asarray(inputs["L0_attn_u"], np.float32))
    in_maps = []
    for i in range(NCORES):
        gcs = np.arange(i, NCH, NCORES)
        sch = gcs % 64
        cosl = np.stack([C["cos"][sc_ * 128:sc_ * 128 + 128] for sc_ in sch])
        sinl = np.stack([C["sin"][sc_ * 128:sc_ * 128 + 128] for sc_ in sch])
        in_maps.append({
            "x0": x0_shards[i],
            "wbr": np.ascontiguousarray(inputs["L0_attn_W_b_rank"], np.float32),
            "wbs": np.ascontiguousarray(inputs["L0_attn_W_b_state"], np.float32),
            "wcr": np.ascontiguousarray(inputs["L0_attn_W_c_rank"], np.float32),
            "wcm": np.ascontiguousarray(inputs["L0_attn_W_c_model"], np.float32),
            "dvec": np.ascontiguousarray(inputs["L0_attn_D_vec"], np.float32).reshape(1, D),
            "afac": np.ascontiguousarray(inputs["L0_attn_A_factors"], np.float32).reshape(1, 32),
            "identD": C["ident"], "onesr": C["ones_row_f32"],
            "wg": np.ascontiguousarray(inputs["L0_ffn_w_gate"]),
            "wu": np.ascontiguousarray(inputs["L0_ffn_w_up"]),
            "wd": np.ascontiguousarray(inputs["L0_ffn_w_down"]),
            "sg": np.ascontiguousarray(inputs["L0_ffn_s_gate"], np.float32).reshape(64, 128),
            "su": np.ascontiguousarray(inputs["L0_ffn_s_up"], np.float32).reshape(64, 128),
            "sd": np.ascontiguousarray(inputs["L0_ffn_s_down"], np.float32).reshape(1, D),
            "identB": C["ident"], "identF": identF,
            "wqkv": np.ascontiguousarray(inputs["L1_attn_w_qkv"], np.float32),
            "lnsc": np.ascontiguousarray(inputs["L1_attn_ln_scale"], np.float32).reshape(128, 1),
            "lnbi": np.ascontiguousarray(inputs["L1_attn_ln_bias"], np.float32).reshape(128, 1),
            "cosl": np.ascontiguousarray(cosl), "sinl": np.ascontiguousarray(sinl),
            "masks": _mk_masks(i), "onesc": C["ones_col_bf"],
            "wout": np.ascontiguousarray(inputs["L1_attn_w_out"], np.float32),
            "gw": np.ascontiguousarray(inputs["L1_ffn_gate_w"], np.float32),
            "wg_e": np.ascontiguousarray(inputs["L1_ffn_w_gate"], np.float32),
            "wu_e": np.ascontiguousarray(inputs["L1_ffn_w_up"], np.float32),
            "wd_e": np.ascontiguousarray(inputs["L1_ffn_w_down"], np.float32),
            "swg": np.ascontiguousarray(inputs["L1_ffn_sw_gate"], np.float32),
            "swu": np.ascontiguousarray(inputs["L1_ffn_sw_up"], np.float32),
            "swd": np.ascontiguousarray(inputs["L1_ffn_sw_down"], np.float32),
        })
    res = run_bass_kernel_spmd(_PROGS["fused"], in_maps, core_ids=list(range(NCORES)))
    return [r["x4o"] for r in res.results]


def _mk_masks(core):
    C = _consts()
    out = np.zeros((64, 128, 128), BF)
    for M in range(64):
        sl = _smin_u(M)
        cq = core + 8 * sl
        if cq > M:
            out[M] = 1.0
        elif cq == M:
            out[M] = C["trilT"].astype(BF)
    return out


def _stage4(qT_parts, kT_all, v_all):
    if "s4" not in _PROGS:
        _PROGS["s4"] = build_stage4()
    C = _consts()
    onesr = C["ones_row_f32"]
    in_maps = []
    kT_b = np.ascontiguousarray(kT_all.astype(BF))
    v_b = np.ascontiguousarray(v_all.astype(BF))
    for i in range(NCORES):
        in_maps.append({
            "qT_l": np.ascontiguousarray(qT_parts[i]),
            "kT_a": kT_b, "v_a": v_b,
            "masks": _mk_masks(i),
            "onesc": C["ones_col_bf"], "onesr": onesr,
        })
    res = run_bass_kernel_spmd(_PROGS["s4"], in_maps, core_ids=list(range(NCORES)))
    return [r["aoT"] for r in res.results]


def emit_stage5(X):
    sc, nc = X.sc, X.nc
    aoT = X.di("aoT", [D, TPC], BF16)
    wout = X.di("wout", [D, D], F32)
    x2 = X.di("x2", [CPC, 128, D], F32)
    x3o = X.do("x3", [CPC, 128, D], F32)
    if True:
        sb = X.sb
        ao_sb = sb("ao_sb", [128, 16 * TPC], BF16)
        wo_sb = sb("wo_sb", [128, 16 * D], BF16)
        x2f = sb("x2f", [128, D], F32)
        xpre = sb("xpre", [128, D], F32)
        rs = sb("rs", [128, 4], F32)
        junk = sb("junk", [128, D], F32)
        sumsq = sb("sumsq", [128, 1], F32)
        stat = sb("stat", [128, 4], F32)
        x3t = sb("x3t", [128, D], F32)
        ps = [X.p512(i) for i in range(8)]

        def lda(eng):
            return eng.dma_start(
                out=ao_sb[:].rearrange("p (c t) -> p c t", t=TPC),
                in_=aoT.rearrange("(c p) t -> p c t", p=128))
        sc.step("sync", lda, incs=[("s_w", 16)])
        def ldw(eng):
            return eng.dma_start(
                out=wo_sb[:].rearrange("p (c j) -> p c j", j=D),
                in_=wout.rearrange("(c p) j -> p c j", p=128))
        sc.step("gpsimd", ldw, incs=[("s_w", 16)])

        for ci in range(CPC):
            pb = 0 if ci % 2 == 0 else 4
            for db in range(4):
                for dc in range(16):
                    def mm(eng, ci=ci, db=db, dc=dc, pb=pb):
                        return eng.matmul(
                            ps[pb + db][:, :],
                            ao_sb[:, dc * TPC + ci * 128:dc * TPC + ci * 128 + 128],
                            wo_sb[:, dc * D + db * 512:dc * D + db * 512 + 512],
                            start=(dc == 0), stop=(dc == 15))
                    w = []
                    if db == 0 and dc == 0:
                        w = [("s_w", None), ("s_v", None)]
                    sc.step("tensor", mm, waits=w,
                            incs=[("s_t", 1)] if (db == 3 and dc == 15) else [])
            def ldx2(eng, ci=ci):
                return eng.dma_start(out=x2f[:], in_=x2[ci])
            sc.step("sync", ldx2, waits=[("s_v", None)], incs=[("s_u", 16)])
            for db in range(4):
                def ev(eng, db=db, pb=pb):
                    return eng.scalar_tensor_tensor(
                        xpre[:, 512 * db:512 * db + 512], ps[pb + db][:, :], 1.0,
                        x2f[:, 512 * db:512 * db + 512], ALU.mult, ALU.add,
                        accum_out=rs[:, db:db + 1])
                sc.step("vector", ev,
                        waits=[("s_t", None), ("s_u", None), ("s_s", None)],
                        incs=[("s_v", 1)] if db == 3 else [])
            emit_ln(sc, xpre, rs, 4, junk, sumsq, stat, [x3t[:]],
                    out_waits=[("s_o", None)])
            def stx(eng, ci=ci):
                return eng.dma_start(out=x3o[ci], in_=x3t[:])
            sc.step("sync", stx, waits=[("s_s", None)], incs=[("s_o", 16)])


def build_stage5():
    return _solo_build(emit_stage5)


def _stage5(inputs, ao_parts, x2_shards):
    if "s5" not in _PROGS:
        _PROGS["s5"] = build_stage5()
    in_maps = []
    for i in range(NCORES):
        in_maps.append({
            "aoT": np.ascontiguousarray(ao_parts[i]),
            "wout": np.ascontiguousarray(inputs["L1_attn_w_out"], np.float32),
            "x2": np.ascontiguousarray(x2_shards[i], np.float32),
        })
    res = run_bass_kernel_spmd(_PROGS["s5"], in_maps, core_ids=list(range(NCORES)))
    return [r["x3"] for r in res.results]


def emit_stage6(X):
    sc, nc = X.sc, X.nc
    x3 = X.di("x3", [CPC, 128, D], F32)
    gw = X.di("gw", [D, 8], F32)
    wg_e = X.di("wg_e", [8, D, 1408], F32)
    wu_e = X.di("wu_e", [8, D, 1408], F32)
    wd_e = X.di("wd_e", [8, 1408, D], F32)
    swg = X.di("swg", [D, 2816], F32)
    swu = X.di("swu", [D, 2816], F32)
    swd = X.di("swd", [2816, D], F32)
    identB = X.di("identB", [128, 128], BF16)
    x4o = X.do("x4o", [CPC, 128, D], F32)
    racc = X.ds("racc", [CPC, 128, D], F32)
    if True:
        sb = X.sb
        s_identB = sb("s_identB", [128, 128], BF16)
        s_gw = sb("s_gw", [128, 16 * 8], BF16)
        xt_bf = sb("xt_bf", [128, 2 * D], BF16)
        xT = sb("xT", [128, 16 * 1024], BF16)
        wgu_sb = sb("wgu_sb", [128, 2 * 2 * 16 * 128], BF16)
        wd_sb = sb("wd_sb", [128, 11 * D], BF16)
        hT = sb("hT", [128, 11 * 1024], BF16)
        gtmp = sb("gtmp", [128, 1024], BF16)
        comb = sb("comb", [128, 8 * 9], F32)
        lg = sb("lg", [128, 8], F32)
        lst = sb("lst", [128, 6], F32)
        etile = sb("etile", [128, 2 * D], F32)
        rfin = sb("rfin", [128, D], F32)
        x3f = sb("x3f", [128, D], F32)
        xpre = sb("xpre", [128, D], F32)
        rs = sb("rs", [128, 4], F32)
        junk = sb("junk", [128, D], F32)
        sumsq = sb("sumsq", [128, 1], F32)
        stat = sb("stat", [128, 4], F32)
        x4t = sb("x4t", [128, D], F32)
        pg = X.p1024(0)
        pu = X.p1024(1)
        pd4 = X.psB

        def ld(dst, src):
            def f(eng, dst=dst, src=src):
                return eng.dma_start(out=dst, in_=src)
            return f
        sc.step("sync", ld(s_identB[:], identB), incs=[("s_w", 16)])
        sc.step("gpsimd", ld(s_gw[:].rearrange("p (c j) -> p c j", j=8),
                             gw.rearrange("(c p) j -> p c j", p=128)),
                incs=[("s_w", 16)])

        NBLK = {e: 11 for e in range(10)}
        s6_ph2_t = [0]
        s6_gu_prev = 0
        s6_ph1_done = [0, 0]
        s6_prev_ph1 = [0, 0]
        nhalves = int(os.environ.get("S6_HALVES", "2"))
        for half in range(nhalves):
            tt0 = half * 8
            s6_prev_ph1 = [0, 0]
            for ti in range(8):
                ci = tt0 + ti
                cb = ti % 2
                def ldx(eng, ci=ci, cb=cb):
                    return eng.dma_start(out=xt_bf[:, cb * D:(cb + 1) * D],
                                         in_=x3[ci])
                sc.step("gpsimd", ldx, waits=[("s_t", None)] if (half + ti) else [],
                        incs=[("s_u", 16)])
                for dg in range(4):
                    po_ = pg if dg % 2 == 0 else pu
                    for s4 in range(4):
                        d_ = dg * 4 + s4
                        def tr(eng, ti=ti, cb=cb, d_=d_, s4=s4, po_=po_):
                            return eng.transpose(
                                po_[:].bitcast(BF16)[:, 128 * s4:128 * s4 + 128],
                                xt_bf[:, cb * D + d_ * 128:cb * D + d_ * 128 + 128],
                                s_identB[:])
                        sc.step("tensor", tr,
                                waits=[("s_u", None), ("s_w", None), ("s_v", None),
                                       ("s_s", None)]
                                if s4 == 0 else [],
                                incs=[("s_t", 1)] if s4 == 3 else [])
                    def tev(eng, ti=ti, dg=dg, po_=po_):
                        src = po_[:].bitcast(BF16)[:, 0:512].rearrange(
                            "p (s j) -> p s j", j=128)
                        dst = xT[:].rearrange("p (d t) -> p d t", t=1024)[
                            :, 4 * dg:4 * dg + 4, ti * 128:ti * 128 + 128]
                        return eng.tensor_copy(dst, src)
                    sc.step("vector", tev, waits=[("s_t", None)], incs=[("s_v", 1)])

            for ti in range(8):
                for dc in range(16):
                    def mm(eng, ti=ti, dc=dc):
                        return eng.matmul(
                            pg[:, 0:8],
                            xT[:, dc * 1024 + ti * 128:dc * 1024 + ti * 128 + 128],
                            s_gw[:, dc * 8:dc * 8 + 8],
                            start=(dc == 0), stop=(dc == 15))
                    w = [("s_v", None), ("s_w", None)] if dc == 0 else []
                    sc.step("tensor", mm, waits=w,
                            incs=[("s_t", 1)] if dc == 15 else [])
                _emit_top2(sc, ti, half, pg, lg, lst, comb)

            for e in range(10):
                nbl = NBLK[e]
                shoff = 1408 * (e - 8)
                for wdg in range(2):
                    r0 = [0, 6][wdg]
                    r1 = [6, nbl][wdg]
                    def ldwd(eng, e=e, r0=r0, r1=r1, shoff=shoff):
                        if e < 8:
                            src = wd_e[e].rearrange("(c p) j -> p c j", p=128)[
                                :, r0:r1, :]
                        else:
                            sh0 = shoff // 128
                            src = swd.rearrange("(c p) j -> p c j", p=128)[
                                :, sh0 + r0:sh0 + r1, :]
                        return eng.dma_start(
                            out=wd_sb[:].rearrange("p (c j) -> p c j", j=D)[:, r0:r1],
                            in_=src)
                    w = []
                    if s6_ph2_t[-1] > 0:
                        w = [("s_wd", None), ("s_t", s6_ph2_t[-1])]
                    sc.step("gpsimd", ldwd, waits=w, incs=[("s_wd", 16)])
                s6_nb_t = [0, 0]
                for nb in range(nbl):
                    wb = nb % 2
                    def mk_ldg(e, nb, wb, gu, shoff=0):
                        def f(eng, e=e, nb=nb, wb=wb, gu=gu, shoff=shoff):
                            if e < 8:
                                src0 = (wg_e if gu == 0 else wu_e)[e]
                                off = 128 * nb
                            else:
                                src0 = swg if gu == 0 else swu
                                off = shoff + 128 * nb
                            src = src0.rearrange("(c p) n -> p c n", p=128)[
                                :, :, off:off + 128]
                            return eng.dma_start(
                                out=wgu_sb[:].rearrange(
                                    "p (g b c n) -> p g b c n", g=2, b=2, n=128)[:, gu, wb],
                                in_=src)
                        return f
                    wld = [("s_t", s6_nb_t[nb])] if nb >= 2 else \
                          ([("s_t", s6_gu_prev)] if s6_gu_prev else [])
                    sc.step("gpsimd", mk_ldg(e, nb, wb, 0, shoff),
                            waits=wld, incs=[("s_u", 16)])
                    sc.step("gpsimd", mk_ldg(e, nb, wb, 1, shoff), incs=[("s_u", 16)])
                    for gu in range(2):
                        pgu = pg if gu == 0 else pu
                        for dc in range(16):
                            for tb in range(2):
                                def mm(eng, gu=gu, wb=wb, dc=dc, tb=tb, pgu=pgu):
                                    return eng.matmul(
                                        pgu[:, tb * 512:tb * 512 + 512],
                                        wgu_sb[:, (gu * 2 + wb) * 2048 + dc * 128:
                                               (gu * 2 + wb) * 2048 + dc * 128 + 128],
                                        xT[:, dc * 1024 + tb * 512:dc * 1024 + tb * 512 + 512],
                                        start=(dc == 0), stop=(dc == 15))
                                w = []
                                if gu == 0 and dc == 0 and tb == 0:
                                    w = [("s_u", None), ("s_s", None),
                                         ("s_v", None)]
                                sc.step("tensor", mm, waits=w,
                                        incs=[("s_t", 1)] if (gu == 1 and dc == 15 and tb == 1) else [])
                    def sil(eng):
                        return eng.activation(gtmp[:], pg[:, :], AF.Silu)
                    sc.step("scalar", sil, waits=[("s_t", None), ("s_v", None)],
                            incs=[("s_s", 1)])
                    def hm(eng, nb=nb):
                        return eng.tensor_mul(hT[:, nb * 1024:nb * 1024 + 1024],
                                              pu[:, :], gtmp[:])
                    sc.step("vector", hm, waits=[("s_s", None), ("s_t", None)],
                            incs=[("s_v", 1)])
                    s6_nb_t.append(sc.count("s_t"))
                s6_ph1_done[0] = sc.count("s_s")
                s6_ph1_done[1] = sc.count("s_v")
                for ti in range(8):
                    eb = ti % 2
                    for nc_ in range(nbl):
                        for db in range(4):
                            def mm(eng, nc_=nc_, ti=ti, db=db, nbl=nbl):
                                return eng.matmul(
                                    pd4[:, db * 512:db * 512 + 512],
                                    hT[:, nc_ * 1024 + ti * 128:nc_ * 1024 + ti * 128 + 128],
                                    wd_sb[:, nc_ * D + db * 512:nc_ * D + db * 512 + 512],
                                    start=(nc_ == 0), stop=(nc_ == nbl - 1))
                            w = []
                            if nc_ == 0 and db == 0:
                                w = [("s_wd", None), ("s_v", None), ("s_s", None)]
                            sc.step("tensor", mm, waits=w,
                                    incs=[("s_t", 1)] if (nc_ == nbl - 1 and db == 3) else [])
                    for db in range(4):
                        def evs(eng, e=e, ti=ti, db=db, eb=eb, half=half):
                            if e < 8:
                                return eng.activation(
                                    etile[:, eb * D + db * 512:eb * D + db * 512 + 512],
                                    pd4[:, db * 512:db * 512 + 512], AF.Identity,
                                    scale=comb[:, ti * 8 + e:ti * 8 + e + 1])
                            return eng.activation(
                                etile[:, eb * D + db * 512:eb * D + db * 512 + 512],
                                pd4[:, db * 512:db * 512 + 512], AF.Identity)
                        sc.step("scalar", evs,
                                waits=[("s_t", None), ("s_acc", None), ("s_v", None)],
                                incs=[("s_s", 1)] if db == 3 else [])
                    def acc(eng, e=e, ti=ti, eb=eb, half=half):
                        return eng.dma_start(
                            out=racc[half * 8 + ti],
                            in_=etile[:, eb * D:(eb + 1) * D],
                            accum_op=(ALU.bypass if e == 0 else ALU.add))
                    accw = [("s_s", None)]
                    if e > 0:
                        accw.append(("s_acc", sc.count("s_acc") - 112))
                    sc.step("gpsimd", acc, waits=accw, incs=[("s_acc", 16)])
                s6_ph2_t.append(sc.count("s_t"))
                s6_gu_prev = s6_nb_t[-1]
                s6_prev_ph1 = list(s6_ph1_done)

            for ti in range(8):
                ci = tt0 + ti
                def ldr(eng, ci=ci):
                    return eng.dma_start(out=rfin[:], in_=racc[ci])
                sc.step("sync", ldr,
                        waits=[("s_acc", sc.count("s_acc")), ("s_v", None)],
                        incs=[("s_r", 16)])
                def ldx3(eng, ci=ci):
                    return eng.dma_start(out=x3f[:], in_=x3[ci])
                sc.step("sync", ldx3, incs=[("s_r", 16)])
                def ev(eng):
                    return eng.scalar_tensor_tensor(
                        xpre[:], rfin[:], 1.0, x3f[:], ALU.mult, ALU.add,
                        accum_out=rs[:, 0:1])
                sc.step("vector", ev, waits=[("s_r", None), ("s_s", None)],
                        incs=[("s_v", 1)])
                emit_ln(sc, xpre, rs, 1, junk, sumsq, stat, [x4t[:]],
                        out_waits=[("s_o", None)])
                def stx(eng, ci=ci):
                    return eng.dma_start(out=x4o[ci], in_=x4t[:])
                sc.step("sync", stx, waits=[("s_s", None)], incs=[("s_o", 16)])


def build_stage6():
    return _solo_build(emit_stage6)


def _emit_top2(sc, ti, half, pg, lg, lst, comb):
    def ev(eng):
        return eng.tensor_copy(lg[:], pg[:, 0:8])
    sc.step("vector", ev, waits=[("s_t", None), ("s_s", None)], incs=[("s_v", 1)])
    def m1(eng):
        return eng.reduce_max(lst[:, 0:1], lg[:], AXX)
    sc.step("vector", m1, waits=[])
    def nm(eng):
        return eng.tensor_scalar_mul(lst[:, 0:1], lst[:, 0:1], -1.0)
    sc.step("vector", nm, waits=[], incs=[("s_v", 1)])
    def ex(eng):
        return eng.activation(lg[:], lg[:], AF.Exp, bias=lst[:, 0:1])
    sc.step("scalar", ex, waits=[("s_v", None)], incs=[("s_s", 1)])
    def sm(eng):
        return eng.reduce_sum(lst[:, 1:2], lg[:], AXX)
    sc.step("vector", sm, waits=[("s_s", None)])
    def rc(eng):
        return eng.reciprocal(lst[:, 1:2], lst[:, 1:2])
    sc.step("vector", rc, waits=[])
    def sco(eng):
        return eng.tensor_scalar_mul(lg[:], lg[:], lst[:, 1:2])
    sc.step("vector", sco, waits=[])
    def mx(eng):
        return eng.reduce_max(lst[:, 2:3], lg[:], AXX)
    sc.step("vector", mx, waits=[])
    def eq(eng):
        return eng.tensor_scalar(comb[:, ti * 8:ti * 8 + 8], lg[:],
                                 lst[:, 2:3], -1e30, ALU.is_ge, ALU.mult)
    sc.step("vector", eq, waits=[])
    def tmp(eng):
        return eng.tensor_add(comb[:, ti * 8:ti * 8 + 8],
                              comb[:, ti * 8:ti * 8 + 8], lg[:])
    sc.step("vector", tmp, waits=[])
    def m2(eng):
        return eng.reduce_max(lst[:, 3:4], comb[:, ti * 8:ti * 8 + 8], AXX)
    sc.step("vector", m2, waits=[])
    def kp(eng):
        return eng.tensor_scalar(comb[:, ti * 8:ti * 8 + 8], lg[:],
                                 lst[:, 3:4], None, ALU.is_ge)
    sc.step("vector", kp, waits=[])
    def cb(eng):
        return eng.tensor_mul(comb[:, ti * 8:ti * 8 + 8],
                              comb[:, ti * 8:ti * 8 + 8], lg[:])
    sc.step("vector", cb, waits=[], incs=[("s_v", 1)])


def _stage6(inputs, x3_shards):
    if "s6" not in _PROGS:
        _PROGS["s6"] = build_stage6()
    C = _consts()
    in_maps = []
    for i in range(NCORES):
        in_maps.append({
            "x3": np.ascontiguousarray(x3_shards[i], np.float32),
            "gw": np.ascontiguousarray(inputs["L1_ffn_gate_w"], np.float32),
            "wg_e": np.ascontiguousarray(inputs["L1_ffn_w_gate"], np.float32),
            "wu_e": np.ascontiguousarray(inputs["L1_ffn_w_up"], np.float32),
            "wd_e": np.ascontiguousarray(inputs["L1_ffn_w_down"], np.float32),
            "swg": np.ascontiguousarray(inputs["L1_ffn_sw_gate"], np.float32),
            "swu": np.ascontiguousarray(inputs["L1_ffn_sw_up"], np.float32),
            "swd": np.ascontiguousarray(inputs["L1_ffn_sw_down"], np.float32),
            "identB": C["ident"],
        })
    res = run_bass_kernel_spmd(_PROGS["s6"], in_maps, core_ids=list(range(NCORES)))
    return [r["x4o"] for r in res.results]


def _stage2(inputs, x1_shards):
    if "s2" not in _PROGS:
        _PROGS["s2"] = build_stage2()
    C = _consts()
    identF = np.eye(128, dtype=np.float32)
    in_maps = []
    for i in range(NCORES):
        in_maps.append({
            "x1": np.ascontiguousarray(x1_shards[i], np.float32),
            "wg": np.ascontiguousarray(inputs["L0_ffn_w_gate"]),
            "wu": np.ascontiguousarray(inputs["L0_ffn_w_up"]),
            "wd": np.ascontiguousarray(inputs["L0_ffn_w_down"]),
            "sg": np.ascontiguousarray(inputs["L0_ffn_s_gate"], np.float32).reshape(64, 128),
            "su": np.ascontiguousarray(inputs["L0_ffn_s_up"], np.float32).reshape(64, 128),
            "sd": np.ascontiguousarray(inputs["L0_ffn_s_down"], np.float32).reshape(1, D),
            "identB": C["ident"], "identF": identF, "onesr": C["ones_row_f32"],
        })
    res = run_bass_kernel_spmd(_PROGS["s2"], in_maps, core_ids=list(range(NCORES)))
    return [r["x2"] for r in res.results]


_PROGS = {}


def _stage1(inputs, x0_shards):
    if "s1" not in _PROGS:
        _PROGS["s1"] = build_stage1()
    C = _consts()
    in_maps = []
    for i in range(NCORES):
        in_maps.append({
            "x0": x0_shards[i],
            "wbr": np.ascontiguousarray(inputs["L0_attn_W_b_rank"], np.float32),
            "wbs": np.ascontiguousarray(inputs["L0_attn_W_b_state"], np.float32),
            "wcr": np.ascontiguousarray(inputs["L0_attn_W_c_rank"], np.float32),
            "wcm": np.ascontiguousarray(inputs["L0_attn_W_c_model"], np.float32),
            "dvec": np.ascontiguousarray(inputs["L0_attn_D_vec"], np.float32).reshape(1, D),
            "afac": np.ascontiguousarray(inputs["L0_attn_A_factors"], np.float32).reshape(1, 32),
            "identD": C["ident"], "onesr": C["ones_row_f32"],
        })
    res = run_bass_kernel_spmd(_PROGS["s1"], in_maps, core_ids=list(range(NCORES)))
    return [r["x1"] for r in res.results]


def _stage3(inputs, x2_shards):
    if "s3" not in _PROGS:
        _PROGS["s3"] = build_stage3()
    C = _consts()
    in_maps = []
    for i in range(NCORES):
        gcs = np.arange(i, NCH, NCORES)
        sch = gcs % 64
        cosl = np.stack([C["cos"][sc_ * 128:sc_ * 128 + 128] for sc_ in sch])
        sinl = np.stack([C["sin"][sc_ * 128:sc_ * 128 + 128] for sc_ in sch])
        in_maps.append({
            "x2": np.ascontiguousarray(x2_shards[i], np.float32),
            "wqkv": np.ascontiguousarray(inputs["L1_attn_w_qkv"], np.float32),
            "lnsc": np.ascontiguousarray(inputs["L1_attn_ln_scale"], np.float32).reshape(128, 1),
            "lnbi": np.ascontiguousarray(inputs["L1_attn_ln_bias"], np.float32).reshape(128, 1),
            "identB": C["ident"],
            "cosl": np.ascontiguousarray(cosl), "sinl": np.ascontiguousarray(sinl),
        })
    res = run_bass_kernel_spmd(_PROGS["s3"], in_maps, core_ids=list(range(NCORES)))
    qT = [r["qT_l"] for r in res.results]
    kT = [np.asarray(r["kv"])[0:D] for r in res.results]
    vO = [np.asarray(r["kv"])[D:].reshape(CPC, 128, D) for r in res.results]
    return qT, kT, vO


def _assemble_heads(qT_parts):
    out = np.zeros((B, 16, 128, SEQ), np.float32)
    for i in range(NCORES):
        q = np.asarray(qT_parts[i], dtype=np.float32).reshape(16, 128, CPC, 128)
        for lc in range(CPC):
            gc = i + 8 * lc
            b, cq = gc // 64, gc % 64
            out[b, :, :, cq * 128:cq * 128 + 128] = q[:, :, lc]
    return out


KERNEL_STAGES = int(os.environ.get('KSTAGES', '0'))


def _assemble_v(v_parts):
    out = np.zeros((B, 16, SEQ, 128), BF)
    for i in range(NCORES):
        v = np.asarray(v_parts[i]).reshape(CPC, 128, 16, 128)
        for lc in range(CPC):
            gc = i + 8 * lc
            b, cq = gc // 64, gc % 64
            out[b, :, cq * 128:cq * 128 + 128, :] = v[lc].transpose(1, 0, 2)
    return out


def _assemble_kT(kT_parts):
    out = np.zeros((B, 16, 128, SEQ), BF)
    for i in range(NCORES):
        k = np.asarray(kT_parts[i]).reshape(16, 128, CPC, 128)
        for lc in range(CPC):
            gc = i + 8 * lc
            b, cq = gc // 64, gc % 64
            out[b, :, :, cq * 128:cq * 128 + 128] = k[:, :, lc]
    return out


def kernel(**inputs):
    if KERNEL_STAGES == 0:
        x4_shards = _run_fused(inputs)
        return _unshard_tokens(x4_shards)
    x0_shards = _shard_tokens(np.asarray(inputs["L0_attn_u"], np.float32))
    x1_shards = _stage1(inputs, x0_shards)
    if KERNEL_STAGES == 1:
        return _unshard_tokens(x1_shards)
    x2_shards = _stage2(inputs, x1_shards)
    if KERNEL_STAGES == 2:
        return _unshard_tokens(x2_shards)
    qT, kT, vO = _stage3(inputs, x2_shards)
    kT_all = _assemble_kT(kT)
    v_all = _assemble_v(vO)
    ao = _stage4(qT, kT_all, v_all)
    x3_shards = _stage5(inputs, ao, x2_shards)
    if KERNEL_STAGES == 5:
        return _unshard_tokens(x3_shards)
    x4_shards = _stage6(inputs, x3_shards)
    return _unshard_tokens(x4_shards)
